# version 29
# baseline (speedup 1.0000x reference)
import numpy as np
import ml_dtypes

import concourse.bass as bass
import concourse.mybir as mybir
from concourse.bass_utils import run_bass_kernel_spmd

F32 = mybir.dt.float32
F16 = mybir.dt.float16
BF16 = mybir.dt.bfloat16
F8 = mybir.dt.float8e4
AF = mybir.ActivationFunctionType
ALU = mybir.AluOpType
PM = mybir.MatmulPerfMode

D = 1024
T = 2048
B = 8
HD = 64
NH = 16
KT = 8
H4 = 4096
HKT = 32
NTT = 16
TC = 4
EPS = 1e-5

ENGINES = ("tensor", "scalar", "vector", "sync", "gpsimd")


class Builder:

    def __init__(self, sem_names):
        self.streams = {e: [] for e in ENGINES}
        self.count = {s: 0 for s in sem_names}

    def op(self, eng, fn, incs=()):
        for s, a in incs:
            self.count[s] += a
        self.streams[eng].append(("op", fn, tuple(incs)))
        return {s: self.count[s] for s in self.count}

    def wait(self, eng, sem, value):
        assert value <= self.count[sem], (sem, value, self.count[sem])
        self.streams[eng].append(("wait", sem, value))

    def now(self, sem):
        return self.count[sem]

    def mark(self):
        return {e: len(self.streams[e]) for e in ENGINES}

    def emit(self, eng_name, eng, sems, lo, hi):
        for item in self.streams[eng_name][lo[eng_name]:hi[eng_name]]:
            if item[0] == "wait":
                _, s, v = item
                if v > 0:
                    eng.wait_ge(sems[s], v)
            else:
                _, fn, incs = item
                inst = fn(eng)
                for s, a in incs:
                    assert inst is not None
                    inst = inst.then_inc(sems[s], a)


def _host_tables():
    bfdt = ml_dtypes.bfloat16
    theta = np.exp(
        np.float32(np.log(np.float32(10000.0)))
        * np.arange(HD // 2, dtype=np.float32) / np.float32(HD)
    ).astype(np.float32)
    t = np.arange(T, dtype=np.float32)
    args = (t[:, None] * theta[None, :]).astype(np.float32)
    c = np.cos(args.astype(np.float64))
    s = np.sin(args.astype(np.float64))
    sc = np.float64(HD) ** -0.25
    p = np.arange(128)
    i = p % 32
    sign = np.where((p % 64) < 32, -1.0, 1.0)
    cosA = (c[:, i].T * sc).astype(np.float32).astype(bfdt)
    sinB = (s[:, i].T * sc * sign[:, None]).astype(np.float32).astype(bfdt)
    tri = np.triu(np.ones((128, 128), np.float32)).astype(bfdt)
    keep = _bernoulli_threefry(42, 0.8, (B, T, D))
    f8dt = ml_dtypes.float8_e4m3
    maskv = np.where(keep, f8dt(1.25), f8dt(0.0))
    return cosA, sinB, tri, maskv


def _threefry2x32(k0, k1, x0, x1):
    def rotl(x, d):
        return ((x << np.uint32(d)) | (x >> np.uint32(32 - d))).astype(np.uint32)
    ks0, ks1 = np.uint32(k0), np.uint32(k1)
    ks2 = np.uint32(ks0 ^ ks1 ^ np.uint32(0x1BD11BDA))
    rotations = [(13, 15, 26, 6), (17, 29, 16, 24)]
    x0 = (x0 + ks0).astype(np.uint32)
    x1 = (x1 + ks1).astype(np.uint32)
    ks = [ks0, ks1, ks2]
    for i in range(5):
        for r in rotations[i % 2]:
            x0 = (x0 + x1).astype(np.uint32)
            x1 = rotl(x1, r)
            x1 = (x1 ^ x0).astype(np.uint32)
        x0 = (x0 + ks[(i + 1) % 3]).astype(np.uint32)
        x1 = (x1 + ks[(i + 2) % 3] + np.uint32(i + 1)).astype(np.uint32)
    return x0, x1


def _bernoulli_threefry(seed, p, shape):
    n = int(np.prod(shape))
    idx = np.arange(n, dtype=np.uint64)
    c1 = (idx >> np.uint64(32)).astype(np.uint32)
    c2 = (idx & np.uint64(0xFFFFFFFF)).astype(np.uint32)
    b1, b2 = _threefry2x32(np.uint32(0), np.uint32(seed), c1, c2)
    bits = (b1 ^ b2).astype(np.uint32)
    fb = (bits >> np.uint32(9)) | np.float32(1.0).view(np.uint32)
    u = fb.view(np.float32) - np.float32(1.0)
    return (u < np.float32(p)).reshape(shape)


def build_nc():
    nc = bass.Bass("TRN2", target_bir_lowering=False)

    xT = nc.dram_tensor("xT", [D, T], F32, kind="ExternalInput").ap()
    wqp = nc.dram_tensor("wqp", [D, D], F32, kind="ExternalInput").ap()
    wkp = nc.dram_tensor("wkp", [D, D], F32, kind="ExternalInput").ap()
    wvp = nc.dram_tensor("wvp", [D, D], F32, kind="ExternalInput").ap()
    wop = nc.dram_tensor("wop", [D, D], F32, kind="ExternalInput").ap()
    w1p = nc.dram_tensor("w1p", [D, H4], F32, kind="ExternalInput").ap()
    w2p = nc.dram_tensor("w2p", [H4, D], F32, kind="ExternalInput").ap()
    b1m = nc.dram_tensor("b1m", [128, 32], F32, kind="ExternalInput").ap()
    b2m = nc.dram_tensor("b2m", [128, 8], F32, kind="ExternalInput").ap()
    cosA = nc.dram_tensor("cosA", [128, T], BF16, kind="ExternalInput").ap()
    sinB = nc.dram_tensor("sinB", [128, T], BF16, kind="ExternalInput").ap()
    triI = nc.dram_tensor("triI", [128, 128], BF16, kind="ExternalInput").ap()
    maskT = nc.dram_tensor("maskT", [D, T], F8, kind="ExternalInput").ap()
    cstI = nc.dram_tensor("cstI", [128, 8], F32, kind="ExternalInput").ap()
    yT = nc.dram_tensor("yT", [D, T], F32, kind="ExternalOutput").ap()

    from contextlib import ExitStack
    outer = ExitStack()
    E_ = outer.enter_context

    cst = E_(nc.sbuf_tensor("cst", [128, 8], F32))
    onesf = E_(nc.sbuf_tensor("onesf", [128, 128], F32))
    onesb = E_(nc.sbuf_tensor("onesb", [128, 1], BF16))
    onesr = E_(nc.sbuf_tensor("onesr", [128, 128], BF16))
    scr = E_(nc.sbuf_tensor("scr", [128, T], F32))
    avn = E_(nc.sbuf_tensor("avn", [128, KT * T], BF16))
    wvo = E_(nc.sbuf_tensor("wvo", [128, KT * D], BF16))
    b1q = E_(nc.sbuf_tensor("b1q", [128, 32], F8))
    b1f = E_(nc.sbuf_tensor("b1f", [128, 32], F32))
    b2q = E_(nc.sbuf_tensor("b2q", [128, 8], F8))
    b2f = E_(nc.sbuf_tensor("b2f", [128, 8], F32))

    psA = E_(nc.psum_tensor("psA", [128, 2048], F32))
    psB = E_(nc.psum_tensor("psB", [128, 2048], F32))

    nc.const_aps.register(F32, EPS, cst[:, 0:1])

    sem_names = (["tPE", "tACT", "tDVE", "sGC", "sGX", "sGV", "sGQ", "sGK",
                  "sGW1", "sGW2", "sSwQ", "sSwK"]
                 + [f"xr{j}" for j in range(4)]
                 + [f"mk{j}" for j in range(8)]
                 + [f"ot{j}" for j in range(4)])
    sems = {s: E_(nc.semaphore(s)) for s in sem_names}
    Bld = Builder(sem_names)

    attn = ExitStack()
    A_ = attn.enter_context
    xbf = A_(nc.sbuf_tensor("xbf", [128, KT * T], BF16))
    wqb = A_(nc.sbuf_tensor("wqb", [128, KT * D], BF16))
    wkb = A_(nc.sbuf_tensor("wkb", [128, KT * D], BF16))
    vag = A_(nc.sbuf_tensor("vag", [128, NTT * (NH * 65)], BF16))
    cosS = A_(nc.sbuf_tensor("cosS", [128, T], BF16))
    sinS = A_(nc.sbuf_tensor("sinS", [128, T], BF16))
    triS = A_(nc.sbuf_tensor("triS", [128, 128], BF16))
    qsbP = [A_(nc.sbuf_tensor("qsbA", [128, T], BF16)),
            A_(nc.sbuf_tensor("qsbB", [128, T], BF16))]
    ksbP = [A_(nc.sbuf_tensor("ksbA", [128, T], BF16)),
            A_(nc.sbuf_tensor("ksbB", [128, T], BF16))]
    qsw = A_(nc.sbuf_tensor("qsw", [128, T], BF16))
    ksw = A_(nc.sbuf_tensor("ksw", [128, T], BF16))
    tmpv = A_(nc.sbuf_tensor("tmpv", [128, T], BF16))
    ppair = A_(nc.sbuf_tensor("ppair", [128, T], BF16))


    def dma(eng_name, sem, dst, src):
        Bld.op(eng_name, lambda e, d=dst, s=src: e.dma_start(out=d, in_=s),
               incs=[(sem, 16)])

    dma("gpsimd", "sGC", cst[:], cstI)
    dma("gpsimd", "sGC", triS[:], triI)
    dma("gpsimd", "sGC", cosS[:], cosA)
    dma("gpsimd", "sGC", sinS[:], sinB)
    dma("gpsimd", "sGC", b1q[:], b1m)
    dma("gpsimd", "sGC", b2q[:], b2m)
    dma("gpsimd", "sGX", xbf[:].rearrange("p (k t) -> p k t", k=KT)[:, 0:4, :],
        xT.rearrange("(k p) t -> p k t", p=128)[:, 0:4, :])
    dma("gpsimd", "sGX", xbf[:].rearrange("p (k t) -> p k t", k=KT)[:, 4:8, :],
        xT.rearrange("(k p) t -> p k t", p=128)[:, 4:8, :])
    dma("gpsimd", "sGV", wvo[:].rearrange("p (k d) -> p k d", k=KT),
        wvp.rearrange("(k p) d -> p k d", p=128))
    dma("gpsimd", "sGQ", wqb[:].rearrange("p (k d) -> p k d", k=KT),
        wqp.rearrange("(k p) d -> p k d", p=128))
    dma("gpsimd", "sGK", wkb[:].rearrange("p (k d) -> p k d", k=KT),
        wkp.rearrange("(k p) d -> p k d", p=128))

    Bld.wait("vector", "sGC", 96)
    Bld.op("vector", lambda e: e.memset(onesf[:], 1.0))
    Bld.op("vector", lambda e: e.memset(onesb[:], 1.0))
    Bld.op("vector", lambda e: e.memset(onesr[:], 1.0))
    Bld.op("vector", lambda e: e.memset(
        vag[:].rearrange("p (n g) -> p n g", g=65)[:, :, 64:65], 1.0))
    Bld.op("vector", lambda e: e.tensor_copy(b1f[:], b1q[:]))
    Bld.op("vector", lambda e: e.tensor_copy(b2f[:], b2q[:]))

    prev_avnm = 0
    prev_avev_dve = 0
    state = {}
    H = {}

    def build_trickle(g):
        steps = []
        for which in ("q", "k"):
            for c in range(4):
                for k in range(KT):
                    steps.append(("mm", which, c, k))
                steps.append(("evict", which, c))
            steps.append(("swap", which))
            steps.append(("rope", which))
        return steps

    def pump_trickle(g, steps, n_mm):
        qdst = qsbP[g % 2]
        kdst = ksbP[g % 2]
        used = 0
        while steps and n_mm > 0:
            kind = steps[0][0]
            if kind == "mm":
                _, which, c, k = steps.pop(0)
                wsel = wqb if which == "q" else wkb
                bank = 1024 + (c % 2) * 512
                if k == 0:
                    if which == "q":
                        pk = (g, "q", c - 2) if c >= 2 else (g - 1, "k", c + 2)
                    else:
                        pk = (g, "k", c - 2) if c >= 2 else (g, "q", c + 2)
                    if pk in state:
                        Bld.wait("tensor", "tDVE", state[pk]["tDVE"])
                def mm(e, k=k, c=c, g=g, bank=bank, wsel=wsel):
                    return e.matmul(psB[:, bank: bank + 512],
                                    wsel[:, k * D + g * 128: k * D + g * 128 + 128],
                                    xbf[:, k * T + c * 512: k * T + c * 512 + 512],
                                    start=(k == 0), stop=(k == KT - 1))
                L = Bld.op("tensor", mm, incs=[("tPE", 1)] if k == KT - 1 else [])
                if k == KT - 1:
                    state[(g, which, c, "mm")] = L
                n_mm -= 1
            elif kind == "evict":
                _, which, c = steps.pop(0)
                bank = 1024 + (c % 2) * 512
                Bld.wait("vector", "tPE", state[(g, which, c, "mm")]["tPE"])
                def ev(e, c=c, bank=bank, dst=(qdst if which == "q" else kdst)):
                    return e.tensor_copy(dst[:, c * 512:(c + 1) * 512], psB[:, bank: bank + 512])
                state[(g, which, c)] = Bld.op("vector", ev, incs=[("tDVE", 1)])
            elif kind == "swap":
                _, which = steps.pop(0)
                dst = qdst if which == "q" else kdst
                sw = qsw if which == "q" else ksw
                sem = "sSwQ" if which == "q" else "sSwK"
                Bld.wait("sync", "tDVE", state[(g, which, 3)]["tDVE"])
                for (do, di) in [(0, 32), (32, 0), (64, 96), (96, 64)]:
                    dma("sync", sem, sw[do:do + 32, :], dst[di:di + 32, :])
                state[(g, which, "sw")] = Bld.now(sem)
            else:
                _, which = steps.pop(0)
                dst = qdst if which == "q" else kdst
                sw = qsw if which == "q" else ksw
                sem = "sSwQ" if which == "q" else "sSwK"
                Bld.wait("vector", sem, state[(g, which, "sw")])
                Bld.op("vector", lambda e, d=dst: e.tensor_mul(d[:], d[:], cosS[:]))
                Bld.op("vector", lambda e, sw=sw: e.tensor_mul(tmpv[:], sw[:], sinS[:]))
                state[(g, which + "ro")] = Bld.op(
                    "vector", lambda e, d=dst: e.tensor_add(d[:], d[:], tmpv[:]),
                    incs=[("tDVE", 1)])
            used += 1
        return used


    Bld.wait("tensor", "sGX", 32)
    Bld.wait("tensor", "sGV", 16)
    vproj = {}
    vevict = {}
    trick0 = build_trickle(0)
    for tt in range(NTT):
        bs = (tt % 2) * 1024
        if tt >= 2:
            Bld.wait("tensor", "tDVE", vevict[tt - 2]["tDVE"])
        for k in range(KT):
            for dc in range(2):
                def mm(e, k=k, dc=dc, bs=bs, tt=tt):
                    return e.matmul(psA[:, bs + dc * 512: bs + dc * 512 + 512],
                                    xbf[:, k * T + tt * 128: k * T + tt * 128 + 128],
                                    wvo[:, k * D + dc * 512: k * D + dc * 512 + 512],
                                    start=(k == 0), stop=(k == KT - 1))
                inc = [("tPE", 1)] if (k == KT - 1 and dc == 1) else []
                vproj[tt] = Bld.op("tensor", mm, incs=inc)
        if tt >= 8:
            Bld.wait("tensor", "sGQ", 16)
            Bld.wait("tensor", "sGK", 16)
            pump_trickle(0, trick0, 8)
        Bld.wait("vector", "tPE", vproj[tt]["tPE"])
        for dc in range(2):
            def ev(e, tt=tt, dc=dc, bs=bs):
                dst = vag[:, tt * 1040: (tt + 1) * 1040]
                dst = dst.rearrange("p (g m) -> p g m", g=16)[:, dc * 8:(dc + 1) * 8, 0:64]
                src = psA[:, bs + dc * 512: bs + dc * 512 + 512]
                src = src.rearrange("p (g m) -> p g m", g=8)
                return e.tensor_copy(dst, src)
            vevict[tt] = Bld.op("vector", ev, incs=[("tDVE", 1)] if dc == 1 else [])

    Bld.wait("gpsimd", "tPE", vproj[NTT - 1]["tPE"])
    dma("gpsimd", "sGV", wvo[:].rearrange("p (k d) -> p k d", k=KT),
        wop.rearrange("(k p) d -> p k d", p=128))


    while trick0:
        pump_trickle(0, trick0, 8)

    for g in range(8):
        qsb = qsbP[g % 2]
        ksb = ksbP[g % 2]
        trick = build_trickle(g + 1) if g < 7 else []
        if g == 0 and trick:
            pump_trickle(1, trick, 24)
        exp_for_buf = {0: None, 1: None}
        av_for_buf = {0: None, 1: None}
        mask_for_buf = {0: None, 1: None}
        Bld.wait("tensor", "tDVE", state[(g, "kro")]["tDVE"])
        Bld.wait("tensor", "tDVE", state[(g, "qro")]["tDVE"])
        Bld.wait("tensor", "sGQ", 16)
        Bld.wait("tensor", "sGK", 16)
        for tq in range(4):
            nlt = 4 * (tq + 1)
            pending_av = None
            for lt in range(nlt):
                dt0 = max(0, lt * 128 - tq * 512)
                W = 512 - dt0
                buf = lt % 2
                is_diag = (dt0 > 0) or (lt == 4 * tq)

                if exp_for_buf[buf] is not None:
                    Bld.wait("tensor", "tACT", exp_for_buf[buf]["tACT"])
                elif buf == 1 and prev_avnm:
                    Bld.wait("tensor", "tDVE", prev_avnm)
                def mmE0(e, lt=lt, tq=tq, dt0=dt0, buf=buf, qsb=qsb, ksb=ksb):
                    return e.matmul(psA[:, buf * 1024 + dt0: buf * 1024 + 512],
                                    ksb[0:64, lt * 128: lt * 128 + 128],
                                    qsb[0:64, tq * 512 + dt0: (tq + 1) * 512],
                                    start=True, stop=True)
                Bld.op("tensor", mmE0)
                def mmE1(e, lt=lt, tq=tq, dt0=dt0, buf=buf, qsb=qsb, ksb=ksb):
                    return e.matmul(psA[:, buf * 1024 + 512 + dt0: buf * 1024 + 1024],
                                    ksb[64:128, lt * 128: lt * 128 + 128],
                                    qsb[64:128, tq * 512 + dt0: (tq + 1) * 512],
                                    start=True, stop=True)
                eL = Bld.op("tensor", mmE1, incs=[("tPE", 1)])

                Bld.wait("scalar", "tPE", eL["tPE"])
                if av_for_buf[buf] is not None:
                    Bld.wait("scalar", "tPE", av_for_buf[buf]["tPE"])
                if mask_for_buf[buf] is not None:
                    Bld.wait("scalar", "tDVE", mask_for_buf[buf]["tDVE"])
                def ex(e, dt0=dt0, buf=buf):
                    return e.activation(
                        ppair[:, buf * 1024: buf * 1024 + 1024].rearrange(
                            "p (h w) -> p h w", h=2)[:, :, dt0:512],
                        psA[:, buf * 1024: buf * 1024 + 1024].rearrange(
                            "p (h w) -> p h w", h=2)[:, :, dt0:512],
                        AF.Exp)
                expL = Bld.op("scalar", ex, incs=[("tACT", 1)])
                exp_for_buf[buf] = expL

                maskL = None
                if is_diag:
                    Bld.wait("vector", "tACT", expL["tACT"])
                    def mk(e, dt0=dt0, buf=buf):
                        view = ppair[:, buf * 1024: buf * 1024 + 1024].rearrange(
                            "p (h w) -> p h w", h=2)[:, :, dt0:dt0 + 128]
                        return e.tensor_tensor(
                            view, view,
                            triS[:].unsqueeze(1).broadcast_to([128, 2, 128]),
                            ALU.mult)
                    maskL = Bld.op("vector", mk, incs=[("tDVE", 1)])
                    mask_for_buf[buf] = maskL

                def emit_av(lt, dt0, buf, expL, maskL):
                    Bld.wait("tensor", "tACT", expL["tACT"])
                    if maskL is not None:
                        Bld.wait("tensor", "tDVE", maskL["tDVE"])
                    if lt == 0 and prev_avev_dve:
                        Bld.wait("tensor", "tDVE", prev_avev_dve)
                    def av0(e, g=g, lt=lt, dt0=dt0, buf=buf, first=(lt == 0)):
                        return e.matmul(psB[0:65, dt0:512],
                                        vag[:, lt * 1040 + (2 * g) * 65: lt * 1040 + (2 * g) * 65 + 65],
                                        ppair[:, buf * 1024 + dt0: buf * 1024 + 512],
                                        start=first, stop=True)
                    Bld.op("tensor", av0)
                    def av1(e, g=g, lt=lt, dt0=dt0, buf=buf, first=(lt == 0)):
                        return e.matmul(psB[0:65, 512 + dt0: 1024],
                                        vag[:, lt * 1040 + (2 * g + 1) * 65: lt * 1040 + (2 * g + 1) * 65 + 65],
                                        ppair[:, buf * 1024 + 512 + dt0: buf * 1024 + 1024],
                                        start=first, stop=True)
                    L = Bld.op("tensor", av1, incs=[("tPE", 1)])
                    av_for_buf[buf] = L
                    return L

                if pending_av is not None:
                    avL = emit_av(*pending_av)
                pending_av = (lt, dt0, buf, expL, maskL)

                if trick:
                    pump_trickle(g + 1, trick, 4 if tq < 2 else 1)

            if pending_av is not None:
                avL = emit_av(*pending_av)
            Bld.wait("vector", "tPE", avL["tPE"])
            avevV = Bld.op("vector", lambda e: e.tensor_copy(scr[0:64, 0:1024], psB[0:64, 0:1024]),
                           incs=[("tDVE", 1)])
            Bld.wait("scalar", "tPE", avL["tPE"])
            Bld.op("scalar", lambda e: e.activation(scr[64:65, 0:1024], psB[64:65, 0:1024], AF.Ln))
            recL = Bld.op("scalar", lambda e: e.activation(
                scr[64:65, 1024:1536].bitcast(BF16), scr[64:65, 0:1024], AF.Exp, scale=-1.0),
                incs=[("tACT", 1)])
            if trick:
                pump_trickle(g + 1, trick, 8)
            Bld.wait("tensor", "tACT", recL["tACT"])
            for c in range(2):
                def bc(e, c=c):
                    return e.matmul(psA[0:64, 1024 + c * 512: 1024 + (c + 1) * 512],
                                    onesr[64:65, 0:64],
                                    scr[64:65, 1024:1536].bitcast(BF16)[:, c * 512:(c + 1) * 512],
                                    start=True, stop=True)
                bcL = Bld.op("tensor", bc, incs=[("tPE", 1)] if c == 1 else [])
            Bld.wait("vector", "tPE", bcL["tPE"])
            off = g * T + tq * 512
            Bld.op("vector", lambda e, off=off: e.tensor_mul(
                avn[0:64, off: off + 512], scr[0:64, 0:512], psA[0:64, 1024:1536]))
            avnmL = Bld.op("vector", lambda e, off=off: e.tensor_mul(
                avn[64:128, off: off + 512], scr[0:64, 512:1024], psA[0:64, 1536:2048]),
                incs=[("tDVE", 1)])
            prev_avnm = avnmL["tDVE"]
            prev_avev_dve = avevV["tDVE"]
            exp_for_buf = {0: None, 1: None}
            av_for_buf = {0: None, 1: None}
            mask_for_buf = {0: None, 1: None}

        while trick:
            pump_trickle(g + 1, trick, 8)
        if g == 6:
            Bld.wait("gpsimd", "tDVE", state[(7, "kro")]["tDVE"])
            for j in range(4):
                def ldw1(e, j=j):
                    return e.dma_start(
                        out=H["w1"][:].rearrange("p (k h) -> p k h", k=KT)[:, 2 * j: 2 * j + 2, :],
                        in_=w1p.rearrange("(k p) h -> p k h", p=128)[:, 2 * j: 2 * j + 2, :])
                Bld.op("gpsimd", ldw1, incs=[("sGW1", 16)])
            for j in range(4):
                def ldw2(e, j=j):
                    return e.dma_start(
                        out=H["w2"][:].rearrange("p (k d) -> p k d", k=HKT)[:, 8 * j: 8 * j + 8, :],
                        in_=w2p.rearrange("(k p) d -> p k d", p=128)[:, 8 * j: 8 * j + 8, :])
                Bld.op("gpsimd", ldw2, incs=[("sGW2", 16)])

    attn_done_dve = prev_avnm
    attn_done_act = Bld.now("tACT")

    cut = Bld.mark()
    attn.close()

    ffn = ExitStack()
    F_ = ffn.enter_context
    w1q8 = F_(nc.sbuf_tensor("w1q8", [128, KT * H4], F8))
    w2q8 = F_(nc.sbuf_tensor("w2q8", [128, HKT * D], F8))
    H["w1"] = w1q8
    H["w2"] = w2q8
    h8 = F_(nc.sbuf_tensor("h8", [128, HKT * 512], F8))
    sslab = F_(nc.sbuf_tensor("sslab", [128, 2 * KT * 512], BF16))
    x1q8 = F_(nc.sbuf_tensor("x1q8", [128, KT * 512], F8))
    xre = F_(nc.sbuf_tensor("xre", [128, 2 * 512], F32))
    mkr = F_(nc.sbuf_tensor("mkr", [128, 8 * 512], F8))
    sqb = F_(nc.sbuf_tensor("sqb", [128, 8 * 512], BF16))
    h16 = F_(nc.sbuf_tensor("h16", [128, 2 * 512], F16))
    y16 = h16
    ymul = F_(nc.sbuf_tensor("ymul", [128, 512], BF16))
    ost = F_(nc.sbuf_tensor("ost", [128, 2 * 512], F32))


    xre_idx = {}
    mkr_idx = {}
    sadd_hist = {}
    proj_hist = {}
    yev_hist = {}
    out_done = {}
    FS = {}

    def slab(tc, dm):
        off = (tc % 2) * KT * 512 + dm * 512
        return sslab[:, off: off + 512]

    def emit_D_proj(tc, dm):
        i = tc * KT + dm
        if i - 2 in sadd_hist:
            Bld.wait("sync", "tDVE", sadd_hist[i - 2]["tDVE"])
        xsem = f"xr{i % 2}"
        def ldx(e, dm=dm, tc=tc, i=i):
            return e.dma_start(
                out=xre[:, (i % 2) * 512:(i % 2 + 1) * 512],
                in_=xT.rearrange("(k p) t -> p k t", p=128)[:, dm, tc * 512:(tc + 1) * 512])
        Bld.op("sync", ldx, incs=[(xsem, 16)])
        xre_idx[i] = (xsem, Bld.now(xsem))

        if tc == 0 and dm == 0:
            Bld.wait("tensor", "sGV", 32)
            Bld.wait("tensor", "tDVE", attn_done_dve)
        if i >= 2:
            Bld.wait("tensor", "tDVE", sadd_hist[i - 2]["tDVE"])
        projL = None
        for k in range(KT):
            def mm(e, k=k, dm=dm, tc=tc):
                return e.matmul(psA[:, (dm % 2) * 512:(dm % 2) * 512 + 512],
                                wvo[:, k * D + dm * 128: k * D + dm * 128 + 128],
                                avn[:, k * T + tc * 512: k * T + tc * 512 + 512],
                                start=(k == 0), stop=(k == KT - 1))
            projL = Bld.op("tensor", mm, incs=[("tPE", 1)] if k == KT - 1 else [])
        proj_hist[i] = projL

    def emit_D_sadd(tc, dm):
        i = tc * KT + dm
        Bld.wait("vector", "tPE", proj_hist[i]["tPE"])
        Bld.wait("vector", *xre_idx[i])
        def sadd(e, dm=dm, i=i, tc=tc):
            return e.tensor_add(slab(tc, dm),
                                psA[:, (dm % 2) * 512:(dm % 2) * 512 + 512],
                                xre[:, (i % 2) * 512:(i % 2 + 1) * 512])
        sadd_hist[i] = Bld.op("vector", sadd, incs=[("tDVE", 1)])

    def emit_D_chunk(tc, dm):
        emit_D_proj(tc, dm)
        emit_D_sadd(tc, dm)

    def emit_D_partials(tc, dm, slot_gate=None):
        i = tc * KT + dm
        if slot_gate is not None:
            Bld.wait("vector", "tPE", slot_gate["tPE"])
        def csq(e, dm=dm, tc=tc):
            return e.tensor_tensor(sqb[:, dm * 512:(dm + 1) * 512],
                                   slab(tc, dm), slab(tc, dm), ALU.mult)
        FS[(tc, "sq", dm)] = Bld.op("vector", csq, incs=[("tDVE", 1)])

    def emit_mask_load(tc, dm):
        i = tc * KT + dm
        if i - 8 in yev_hist:
            Bld.wait("sync", "tDVE", yev_hist[i - 8]["tDVE"])
        msem = f"mk{i % 8}"
        def ldm(e, dm=dm, tc=tc, i=i):
            return e.dma_start(
                out=mkr[:, (i % 8) * 512:(i % 8 + 1) * 512],
                in_=maskT.rearrange("(k p) t -> p k t", p=128)[:, dm, tc * 512:(tc + 1) * 512])
        Bld.op("sync", ldm, incs=[(msem, 16)])
        mkr_idx[i] = (msem, Bld.now(msem))

    def emit_D_stats(tc, gate):
        statsL = None
        for dm in range(KT):
            Bld.wait("tensor", "tDVE", FS[(tc, "sq", dm)]["tDVE"])
            if dm == 0 and gate:
                Bld.wait("tensor", "tDVE", gate)
            def st1(e, dm=dm, tc=tc):
                return e.matmul(psB[0:1, 1024:1536], onesb[:, 0:1],
                                slab(tc, dm),
                                start=(dm == 0), stop=(dm == KT - 1))
            Bld.op("tensor", st1)
            def st2(e, dm=dm):
                return e.matmul(psB[0:1, 1536:2048], onesb[:, 0:1],
                                sqb[:, dm * 512:(dm + 1) * 512],
                                start=(dm == 0), stop=(dm == KT - 1))
            statsL = Bld.op("tensor", st2, incs=[("tPE", 1)])
        FS[(tc, "stats")] = statsL

    def emit_rows(statsL, rows_ps, r):
        Bld.wait("vector", "tPE", statsL["tPE"])
        Bld.op("vector", lambda e: e.tensor_scalar_mul(
            scr[r:r + 1, 0:512], rows_ps[0:1, 1024:1536], 1.0 / D))
        Bld.op("vector", lambda e: e.tensor_mul(
            scr[r:r + 1, 1536:2048], scr[r:r + 1, 0:512], scr[r:r + 1, 0:512]))
        vrow = Bld.op("vector", lambda e: e.scalar_tensor_tensor(
            scr[r:r + 1, 512:1024], rows_ps[0:1, 1536:2048], 1.0 / D, scr[r:r + 1, 1536:2048],
            ALU.mult, ALU.subtract), incs=[("tDVE", 1)])
        mbfL = Bld.op("vector", lambda e: e.tensor_copy(
            scr[r:r + 1, 1280:1536].bitcast(BF16), scr[r:r + 1, 0:512]), incs=[("tDVE", 1)])
        Bld.wait("scalar", "tDVE", vrow["tDVE"])
        Bld.op("scalar", lambda e: e.activation(
            scr[r:r + 1, 1536:2048], scr[r:r + 1, 512:1024], AF.Ln, bias=EPS))
        rstdL = Bld.op("scalar", lambda e: e.activation(
            scr[r:r + 1, 1024:1280].bitcast(BF16), scr[r:r + 1, 1536:2048], AF.Exp, scale=-0.5),
            incs=[("tACT", 1)])
        return vrow, mbfL, rstdL

    def emit_bcast(r, mbfL, rstdL, bc_gate=None):
        Bld.wait("tensor", "tACT", rstdL["tACT"])
        Bld.wait("tensor", "tDVE", mbfL["tDVE"])
        if bc_gate:
            Bld.wait("tensor", "tDVE", bc_gate)
        Bld.op("tensor", lambda e: e.matmul(
            psA[:, 1024:1536], onesr[r:r + 1, 0:128], scr[r:r + 1, 1280:1536].bitcast(BF16),
            start=True, stop=True))
        bcL = Bld.op("tensor", lambda e: e.matmul(
            psA[:, 1536:2048], onesr[r:r + 1, 0:128], scr[r:r + 1, 1024:1280].bitcast(BF16),
            start=True, stop=True),
            incs=[("tPE", 1)])
        return bcL

    Bld.wait("sync", "tDVE", attn_done_dve)
    for dm in range(KT):
        emit_D_chunk(0, dm)
        emit_D_partials(0, dm)
        emit_mask_load(0, dm)
    emit_D_stats(0, None)

    def emit_x1n(tcx, bcL, interleave):
        x1n_k = {}
        x1nV = None
        Bld.wait("vector", "tPE", bcL["tPE"])
        for dm in range(KT):
            def n1(e, dm=dm, tc=tcx):
                return e.tensor_sub(slab(tc, dm),
                                    slab(tc, dm),
                                    psA[:, 1024:1536])
            Bld.op("vector", n1)
            def n2(e, dm=dm, tc=tcx):
                return e.tensor_mul(slab(tc, dm),
                                    slab(tc, dm),
                                    psA[:, 1536:2048])
            x1nV = Bld.op("vector", n2, incs=[("tDVE", 1)])
            Bld.wait("scalar", "tDVE", x1nV["tDVE"])
            def n3(e, dm=dm, tc=tcx):
                return e.copy(x1q8[:, dm * 512:(dm + 1) * 512],
                              slab(tc, dm))
            x1n_k[dm] = Bld.op("scalar", n3, incs=[("tACT", 1)])
            if interleave:
                emit_D_chunk(tcx + 1, dm)
        return x1n_k, x1nV

    _, mb0, rs0 = emit_rows(FS[(0, "stats")], psB, 0)
    bc1L = emit_bcast(0, mb0, rs0)
    x1n_k_cur, x1nV_cur = emit_x1n(0, bc1L, True)
    prev_x2_dve = None

    for tc in range(TC):
        hev = {}
        for hm in range(HKT):
            if tc == 0 and hm == 0:
                Bld.wait("tensor", "sGW1", 64)
            if hm == 0 and tc > 0:
                Bld.wait("tensor", "tDVE", yev_hist[tc * KT - 1]["tDVE"])
            if hm >= 3:
                Bld.wait("tensor", "tDVE", hev[hm - 3]["tDVE"])
            mm1L = None
            for kb in range(KT // 2):
                if hm == 0:
                    Bld.wait("tensor", "tACT", x1n_k_cur[2 * kb + 1]["tACT"])
                def mm(e, kb=kb, hm=hm):
                    lhsT = w1q8[:].rearrange("p (k h) -> p k h", k=KT)[
                        :, 2 * kb: 2 * kb + 2, hm * 128:(hm + 1) * 128]
                    rhs = x1q8[:].rearrange("p (k t) -> p k t", k=KT)[
                        :, 2 * kb: 2 * kb + 2, :]
                    return e.matmul(psB[:, (hm % 3) * 512:(hm % 3) * 512 + 512],
                                    lhsT, rhs, start=(kb == 0), stop=(kb == KT // 2 - 1),
                                    perf_mode=PM.DoubleRow)
                mm1L = Bld.op("tensor", mm, incs=[("tPE", 1)] if kb == KT // 2 - 1 else [])
            Bld.wait("scalar", "tPE", mm1L["tPE"])
            if hm >= 2:
                Bld.wait("scalar", "tDVE", hev[hm - 2]["tDVE"])
            def he1(e, hm=hm):
                return e.activation(h16[:, (hm % 2) * 512:(hm % 2) * 512 + 512],
                                    psB[:, (hm % 3) * 512:(hm % 3) * 512 + 512],
                                    AF.Relu, bias=b1f[:, hm:hm + 1])
            he1L = Bld.op("scalar", he1, incs=[("tACT", 1)])
            Bld.wait("vector", "tACT", he1L["tACT"])
            def he2(e, hm=hm):
                return e.tensor_copy(h8[:, hm * 512:(hm + 1) * 512],
                                     h16[:, (hm % 2) * 512:(hm % 2) * 512 + 512])
            hev[hm] = Bld.op("vector", he2, incs=[("tDVE", 1)])

        stats2L = None
        pend_sq = []
        psA_rows_free = prev_x2_dve if prev_x2_dve is not None else x1nV_cur["tDVE"]

        def emit_stats2(tc, dm, sq2L):
            Bld.wait("tensor", "tDVE", sq2L["tDVE"])
            if dm == 0:
                Bld.wait("tensor", "tDVE", psA_rows_free)
            def st1b(e, dm=dm, tc=tc):
                return e.matmul(psA[0:1, 1024:1536], onesb[:, 0:1],
                                slab(tc, dm),
                                start=(dm == 0), stop=(dm == KT - 1))
            Bld.op("tensor", st1b)
            def st2b(e, dm=dm):
                return e.matmul(psA[0:1, 1536:2048], onesb[:, 0:1],
                                sqb[:, dm * 512:(dm + 1) * 512],
                                start=(dm == 0), stop=(dm == KT - 1))
            L = Bld.op("tensor", st2b, incs=[("tPE", 1)])
            if tc < TC - 1:
                emit_D_partials(tc + 1, dm, slot_gate=L)
            return L

        for dm in range(KT):
            i = tc * KT + dm
            if tc == 0 and dm == 0:
                Bld.wait("tensor", "sGW2", 64)
            if dm == 0:
                Bld.wait("tensor", "tDVE", hev[HKT - 1]["tDVE"])
            if i >= 2:
                Bld.wait("tensor", "tDVE", yev_hist[i - 2]["tDVE"])
            mm2L = None
            for kb in range(HKT // 2):
                def mm(e, kb=kb, dm=dm):
                    lhsT = w2q8[:].rearrange("p (k d) -> p k d", k=HKT)[
                        :, 2 * kb: 2 * kb + 2, dm * 128:(dm + 1) * 128]
                    rhs = h8[:].rearrange("p (k t) -> p k t", k=HKT)[
                        :, 2 * kb: 2 * kb + 2, :]
                    return e.matmul(psB[:, 1024 + (dm % 2) * 512: 1024 + (dm % 2) * 512 + 512],
                                    lhsT, rhs, start=(kb == 0), stop=(kb == HKT // 2 - 1),
                                    perf_mode=PM.DoubleRow)
                mm2L = Bld.op("tensor", mm, incs=[("tPE", 1)] if kb == HKT // 2 - 1 else [])
            Bld.wait("scalar", "tPE", mm2L["tPE"])
            if dm >= 2:
                Bld.wait("scalar", "tDVE", yev_hist[tc * KT + dm - 2]["tDVE"])
            def ye1(e, dm=dm):
                return e.activation(y16[:, (dm % 2) * 512:(dm % 2) * 512 + 512],
                                    psB[:, 1024 + (dm % 2) * 512: 1024 + (dm % 2) * 512 + 512],
                                    AF.Identity, bias=b2f[:, dm:dm + 1])
            ye1L = Bld.op("scalar", ye1, incs=[("tACT", 1)])
            Bld.wait("vector", "tACT", ye1L["tACT"])
            Bld.wait("vector", *mkr_idx[i])
            def ye2(e, dm=dm, i=i):
                return e.tensor_mul(ymul[:],
                                    y16[:, (dm % 2) * 512:(dm % 2) * 512 + 512],
                                    mkr[:, (i % 8) * 512:(i % 8 + 1) * 512])
            Bld.op("vector", ye2)
            def ye3(e, dm=dm, tc=tc):
                return e.tensor_add(slab(tc, dm),
                                    slab(tc, dm),
                                    ymul[:])
            yev_hist[i] = Bld.op("vector", ye3, incs=[("tDVE", 1)])
            if tc < TC - 1:
                emit_mask_load(tc + 1, dm)
            def csq2(e, dm=dm, tc=tc):
                return e.tensor_tensor(sqb[:, dm * 512:(dm + 1) * 512],
                                       slab(tc, dm),
                                       slab(tc, dm), ALU.mult)
            sq2L = Bld.op("vector", csq2, incs=[("tDVE", 1)])
            pend_sq.append((dm, sq2L))
            if dm > 0:
                stats2L = emit_stats2(tc, *pend_sq.pop(0))
        stats2L = emit_stats2(tc, *pend_sq.pop(0))

        if tc < TC - 1:
            emit_D_stats(tc + 1, yev_hist[tc * KT + KT - 1]["tDVE"])

        vr2, mb2, rs2 = emit_rows(stats2L, psA, 32)
        if tc < TC - 1:
            _, mb1, rs1 = emit_rows(FS[(tc + 1, "stats")], psB, 0)
            bc1L = emit_bcast(0, mb1, rs1, bc_gate=vr2["tDVE"])
            if tc + 2 <= TC - 1:
                emit_D_proj(tc + 2, 0)
                emit_D_proj(tc + 2, 1)
            x1n_k_nxt, x1nV_nxt = emit_x1n(tc + 1, bc1L, False)
            bc2L = emit_bcast(32, mb2, rs2, bc_gate=x1nV_nxt["tDVE"])
        else:
            bc2L = emit_bcast(32, mb2, rs2)

        Bld.wait("vector", "tPE", bc2L["tPE"])
        x2L = None
        for dm in range(KT):
            i = tc * KT + dm
            def o1(e, dm=dm, tc=tc):
                return e.tensor_sub(slab(tc, dm),
                                    slab(tc, dm),
                                    psA[:, 1024:1536])
            Bld.op("vector", o1)
            if i >= 2:
                Bld.wait("vector", *out_done[i - 2])
            def o2(e, dm=dm, i=i, tc=tc):
                return e.tensor_mul(ost[:, (i % 2) * 512:(i % 2 + 1) * 512],
                                    slab(tc, dm),
                                    psA[:, 1536:2048])
            x2L = Bld.op("vector", o2, incs=[("tDVE", 1)])
            Bld.wait("sync", "tDVE", x2L["tDVE"])
            def so(e, dm=dm, tc=tc, i=i):
                return e.dma_start(
                    out=yT.rearrange("(k p) t -> p k t", p=128)[:, dm, tc * 512:(tc + 1) * 512],
                    in_=ost[:, (i % 2) * 512:(i % 2 + 1) * 512])
            osem = f"ot{i % 2}"
            Bld.op("sync", so, incs=[(osem, 16)])
            out_done[i] = (osem, Bld.now(osem))
            if tc + 2 <= TC - 1:
                emit_D_sadd(tc + 2, dm)
                if dm + 2 < KT:
                    emit_D_proj(tc + 2, dm + 2)

        if tc < TC - 1:
            x1n_k_cur, x1nV_cur = x1n_k_nxt, x1nV_nxt
        prev_x2_dve = x2L["tDVE"]

    for j in range(4):
        Bld.wait("sync", f"ot{j}", Bld.now(f"ot{j}"))

    end = Bld.mark()
    with nc.Block("attn") as block:
        @block.tensor
        def _(e):
            Bld.emit("tensor", e, sems, {k: 0 for k in cut}, cut)

        @block.scalar
        def _(e):
            Bld.emit("scalar", e, sems, {k: 0 for k in cut}, cut)

        @block.vector
        def _(e):
            Bld.emit("vector", e, sems, {k: 0 for k in cut}, cut)

        @block.sync
        def _(e):
            Bld.emit("sync", e, sems, {k: 0 for k in cut}, cut)

        @block.gpsimd
        def _(e):
            Bld.emit("gpsimd", e, sems, {k: 0 for k in cut}, cut)

    with nc.Block("ffn") as block:
        @block.tensor
        def _(e):
            Bld.emit("tensor", e, sems, cut, end)

        @block.scalar
        def _(e):
            Bld.emit("scalar", e, sems, cut, end)

        @block.vector
        def _(e):
            Bld.emit("vector", e, sems, cut, end)

        @block.sync
        def _(e):
            Bld.emit("sync", e, sems, cut, end)

        @block.gpsimd
        def _(e):
            Bld.emit("gpsimd", e, sems, cut, end)

    ffn.close()
    outer.close()
    return nc


_ROWPERM = np.concatenate(
    [h * 64 + np.concatenate([np.arange(0, 64, 2), np.arange(1, 64, 2)])
     for h in range(NH)])


def kernel(L0_attn_x, L0_attn_wq, L0_attn_wk, L0_attn_wv, L0_attn_wo,
           L0_ffn_w1, L0_ffn_b1, L0_ffn_w2, L0_ffn_b2):
    x = np.asarray(L0_attn_x, np.float32)
    wq = np.asarray(L0_attn_wq, np.float32)
    wk = np.asarray(L0_attn_wk, np.float32)
    wv = np.asarray(L0_attn_wv, np.float32)
    wo = np.asarray(L0_attn_wo, np.float32)
    w1 = np.asarray(L0_ffn_w1, np.float32)
    b1 = np.asarray(L0_ffn_b1, np.float32)
    w2 = np.asarray(L0_ffn_w2, np.float32)
    b2 = np.asarray(L0_ffn_b2, np.float32)

    cosA, sinB, tri, maskv = _host_tables()
    wqp = np.ascontiguousarray(wq[_ROWPERM].T)
    wkp = np.ascontiguousarray(wk[_ROWPERM].T)
    wvp = np.ascontiguousarray(wv.T)
    wop = np.ascontiguousarray(wo.T)
    w1p = np.ascontiguousarray(w1.T)
    w2p = np.ascontiguousarray(w2.T)
    b1m = np.ascontiguousarray(b1.reshape(32, 128).T)
    b2m = np.ascontiguousarray(b2.reshape(8, 128).T)
    cstv = np.zeros((128, 8), np.float32)
    cstv[:, 0] = EPS

    nc = build_nc()
    in_maps = []
    for b in range(B):
        in_maps.append({
            "xT": np.ascontiguousarray(x[b].T),
            "wqp": wqp, "wkp": wkp, "wvp": wvp, "wop": wop,
            "w1p": w1p, "w2p": w2p, "b1m": b1m, "b2m": b2m,
            "cosA": cosA, "sinB": sinB, "triI": tri,
            "maskT": np.ascontiguousarray(maskv[b].T),
            "cstI": cstv,
        })
    res = run_bass_kernel_spmd(nc, in_maps, core_ids=list(range(B)))
    out = np.stack([np.asarray(r["yT"], np.float32).T for r in res.results], axis=0)
    return out


# version 30
# speedup vs baseline: 1.0213x; 1.0213x over previous
import numpy as np
import ml_dtypes

import concourse.bass as bass
import concourse.mybir as mybir
from concourse.bass_utils import run_bass_kernel_spmd

F32 = mybir.dt.float32
F16 = mybir.dt.float16
BF16 = mybir.dt.bfloat16
F8 = mybir.dt.float8e4
AF = mybir.ActivationFunctionType
ALU = mybir.AluOpType
PM = mybir.MatmulPerfMode

D = 1024
T = 2048
B = 8
HD = 64
NH = 16
KT = 8
H4 = 4096
HKT = 32
NTT = 16
TC = 4
EPS = 1e-5

ENGINES = ("tensor", "scalar", "vector", "sync", "gpsimd")


class Builder:

    def __init__(self, sem_names):
        self.streams = {e: [] for e in ENGINES}
        self.count = {s: 0 for s in sem_names}

    def op(self, eng, fn, incs=()):
        for s, a in incs:
            self.count[s] += a
        self.streams[eng].append(("op", fn, tuple(incs)))
        return {s: self.count[s] for s in self.count}

    def wait(self, eng, sem, value):
        assert value <= self.count[sem], (sem, value, self.count[sem])
        self.streams[eng].append(("wait", sem, value))

    def now(self, sem):
        return self.count[sem]

    def mark(self):
        return {e: len(self.streams[e]) for e in ENGINES}

    def emit(self, eng_name, eng, sems, lo, hi):
        for item in self.streams[eng_name][lo[eng_name]:hi[eng_name]]:
            if item[0] == "wait":
                _, s, v = item
                if v > 0:
                    eng.wait_ge(sems[s], v)
            else:
                _, fn, incs = item
                inst = fn(eng)
                for s, a in incs:
                    assert inst is not None
                    inst = inst.then_inc(sems[s], a)


def _host_tables():
    bfdt = ml_dtypes.bfloat16
    theta = np.exp(
        np.float32(np.log(np.float32(10000.0)))
        * np.arange(HD // 2, dtype=np.float32) / np.float32(HD)
    ).astype(np.float32)
    t = np.arange(T, dtype=np.float32)
    args = (t[:, None] * theta[None, :]).astype(np.float32)
    c = np.cos(args.astype(np.float64))
    s = np.sin(args.astype(np.float64))
    sc = np.float64(HD) ** -0.25
    p = np.arange(128)
    i = p % 32
    sign = np.where((p % 64) < 32, -1.0, 1.0)
    cosA = (c[:, i].T * sc).astype(np.float32).astype(bfdt)
    sinB = (s[:, i].T * sc * sign[:, None]).astype(np.float32).astype(bfdt)
    tri = np.triu(np.ones((128, 128), np.float32)).astype(bfdt)
    keep = _bernoulli_threefry(42, 0.8, (B, T, D))
    f8dt = ml_dtypes.float8_e4m3
    maskv = np.where(keep, f8dt(1.25), f8dt(0.0))
    return cosA, sinB, tri, maskv


def _threefry2x32(k0, k1, x0, x1):
    def rotl(x, d):
        return ((x << np.uint32(d)) | (x >> np.uint32(32 - d))).astype(np.uint32)
    ks0, ks1 = np.uint32(k0), np.uint32(k1)
    ks2 = np.uint32(ks0 ^ ks1 ^ np.uint32(0x1BD11BDA))
    rotations = [(13, 15, 26, 6), (17, 29, 16, 24)]
    x0 = (x0 + ks0).astype(np.uint32)
    x1 = (x1 + ks1).astype(np.uint32)
    ks = [ks0, ks1, ks2]
    for i in range(5):
        for r in rotations[i % 2]:
            x0 = (x0 + x1).astype(np.uint32)
            x1 = rotl(x1, r)
            x1 = (x1 ^ x0).astype(np.uint32)
        x0 = (x0 + ks[(i + 1) % 3]).astype(np.uint32)
        x1 = (x1 + ks[(i + 2) % 3] + np.uint32(i + 1)).astype(np.uint32)
    return x0, x1


def _bernoulli_threefry(seed, p, shape):
    n = int(np.prod(shape))
    idx = np.arange(n, dtype=np.uint64)
    c1 = (idx >> np.uint64(32)).astype(np.uint32)
    c2 = (idx & np.uint64(0xFFFFFFFF)).astype(np.uint32)
    b1, b2 = _threefry2x32(np.uint32(0), np.uint32(seed), c1, c2)
    bits = (b1 ^ b2).astype(np.uint32)
    fb = (bits >> np.uint32(9)) | np.float32(1.0).view(np.uint32)
    u = fb.view(np.float32) - np.float32(1.0)
    return (u < np.float32(p)).reshape(shape)


def build_nc():
    nc = bass.Bass("TRN2", target_bir_lowering=False)

    xT = nc.dram_tensor("xT", [D, T], F32, kind="ExternalInput").ap()
    wqp = nc.dram_tensor("wqp", [D, D], F32, kind="ExternalInput").ap()
    wkp = nc.dram_tensor("wkp", [D, D], F32, kind="ExternalInput").ap()
    wvp = nc.dram_tensor("wvp", [D, D], F32, kind="ExternalInput").ap()
    wop = nc.dram_tensor("wop", [D, D], F32, kind="ExternalInput").ap()
    w1p = nc.dram_tensor("w1p", [D, H4], F32, kind="ExternalInput").ap()
    w2p = nc.dram_tensor("w2p", [H4, D], F32, kind="ExternalInput").ap()
    b1m = nc.dram_tensor("b1m", [128, 32], F32, kind="ExternalInput").ap()
    b2m = nc.dram_tensor("b2m", [128, 8], F32, kind="ExternalInput").ap()
    cosA = nc.dram_tensor("cosA", [128, T], BF16, kind="ExternalInput").ap()
    sinB = nc.dram_tensor("sinB", [128, T], BF16, kind="ExternalInput").ap()
    triI = nc.dram_tensor("triI", [128, 128], BF16, kind="ExternalInput").ap()
    maskT = nc.dram_tensor("maskT", [D, T], F8, kind="ExternalInput").ap()
    cstI = nc.dram_tensor("cstI", [128, 8], F32, kind="ExternalInput").ap()
    yT = nc.dram_tensor("yT", [D, T], F32, kind="ExternalOutput").ap()

    from contextlib import ExitStack
    outer = ExitStack()
    E_ = outer.enter_context

    cst = E_(nc.sbuf_tensor("cst", [128, 8], F32))
    onesf = E_(nc.sbuf_tensor("onesf", [128, 128], F32))
    onesb = E_(nc.sbuf_tensor("onesb", [128, 1], BF16))
    onesr = E_(nc.sbuf_tensor("onesr", [128, 128], BF16))
    scr = E_(nc.sbuf_tensor("scr", [128, T], F32))
    avn = E_(nc.sbuf_tensor("avn", [128, KT * T], BF16))
    wvo = E_(nc.sbuf_tensor("wvo", [128, KT * D], BF16))
    b1q = E_(nc.sbuf_tensor("b1q", [128, 32], F8))
    b1f = E_(nc.sbuf_tensor("b1f", [128, 32], F32))
    b2q = E_(nc.sbuf_tensor("b2q", [128, 8], F8))
    b2f = E_(nc.sbuf_tensor("b2f", [128, 8], F32))

    psA = E_(nc.psum_tensor("psA", [128, 2048], F32))
    psB = E_(nc.psum_tensor("psB", [128, 2048], F32))

    nc.const_aps.register(F32, EPS, cst[:, 0:1])

    sem_names = (["tPE", "tACT", "tDVE", "sGC", "sGX", "sGV", "sGQ", "sGK",
                  "sGW1", "sGW2", "sSwQ", "sSwK"]
                 + [f"xr{j}" for j in range(4)]
                 + [f"mk{j}" for j in range(8)]
                 + [f"ot{j}" for j in range(4)])
    sems = {s: E_(nc.semaphore(s)) for s in sem_names}
    Bld = Builder(sem_names)

    attn = ExitStack()
    A_ = attn.enter_context
    xbf = A_(nc.sbuf_tensor("xbf", [128, KT * T], BF16))
    wqb = A_(nc.sbuf_tensor("wqb", [128, KT * D], BF16))
    wkb = A_(nc.sbuf_tensor("wkb", [128, KT * D], BF16))
    vag = A_(nc.sbuf_tensor("vag", [128, NTT * (NH * 65)], BF16))
    cosS = A_(nc.sbuf_tensor("cosS", [128, T], BF16))
    sinS = A_(nc.sbuf_tensor("sinS", [128, T], BF16))
    triS = A_(nc.sbuf_tensor("triS", [128, 128], BF16))
    qsbP = [A_(nc.sbuf_tensor("qsbA", [128, T], BF16)),
            A_(nc.sbuf_tensor("qsbB", [128, T], BF16))]
    ksbP = [A_(nc.sbuf_tensor("ksbA", [128, T], BF16)),
            A_(nc.sbuf_tensor("ksbB", [128, T], BF16))]
    qsw = A_(nc.sbuf_tensor("qsw", [128, T], BF16))
    ksw = A_(nc.sbuf_tensor("ksw", [128, T], BF16))
    tmpv = A_(nc.sbuf_tensor("tmpv", [128, T], BF16))
    ppair = A_(nc.sbuf_tensor("ppair", [128, T], BF16))


    def dma(eng_name, sem, dst, src):
        Bld.op(eng_name, lambda e, d=dst, s=src: e.dma_start(out=d, in_=s),
               incs=[(sem, 16)])

    dma("gpsimd", "sGC", cst[:], cstI)
    dma("gpsimd", "sGC", triS[:], triI)
    dma("gpsimd", "sGC", cosS[:], cosA)
    dma("gpsimd", "sGC", sinS[:], sinB)
    dma("gpsimd", "sGC", b1q[:], b1m)
    dma("gpsimd", "sGC", b2q[:], b2m)
    dma("gpsimd", "sGX", xbf[:].rearrange("p (k t) -> p k t", k=KT)[:, 0:4, :],
        xT.rearrange("(k p) t -> p k t", p=128)[:, 0:4, :])
    dma("gpsimd", "sGX", xbf[:].rearrange("p (k t) -> p k t", k=KT)[:, 4:8, :],
        xT.rearrange("(k p) t -> p k t", p=128)[:, 4:8, :])
    dma("gpsimd", "sGV", wvo[:].rearrange("p (k d) -> p k d", k=KT),
        wvp.rearrange("(k p) d -> p k d", p=128))
    dma("gpsimd", "sGQ", wqb[:].rearrange("p (k d) -> p k d", k=KT),
        wqp.rearrange("(k p) d -> p k d", p=128))
    dma("gpsimd", "sGK", wkb[:].rearrange("p (k d) -> p k d", k=KT),
        wkp.rearrange("(k p) d -> p k d", p=128))

    Bld.wait("vector", "sGC", 96)
    Bld.op("vector", lambda e: e.memset(onesf[:], 1.0))
    Bld.op("vector", lambda e: e.memset(onesb[:], 1.0))
    Bld.op("vector", lambda e: e.memset(onesr[:], 1.0))
    Bld.op("vector", lambda e: e.memset(
        vag[:].rearrange("p (n g) -> p n g", g=65)[:, :, 64:65], 1.0))
    Bld.op("vector", lambda e: e.tensor_copy(b1f[:], b1q[:]))
    Bld.op("vector", lambda e: e.tensor_copy(b2f[:], b2q[:]))

    prev_avnm = 0
    prev_avev_dve = 0
    state = {}
    H = {}

    def build_trickle(g):
        steps = []
        for which in ("q", "k"):
            for c in range(4):
                for k in range(KT):
                    steps.append(("mm", which, c, k))
                steps.append(("evict", which, c))
            steps.append(("swap", which))
            steps.append(("rope", which))
        return steps

    def pump_trickle(g, steps, n_mm):
        qdst = qsbP[g % 2]
        kdst = ksbP[g % 2]
        used = 0
        while steps and n_mm > 0:
            kind = steps[0][0]
            if kind == "mm":
                _, which, c, k = steps.pop(0)
                wsel = wqb if which == "q" else wkb
                bank = 1024 + (c % 2) * 512
                if k == 0:
                    if which == "q":
                        pk = (g, "q", c - 2) if c >= 2 else (g - 1, "k", c + 2)
                    else:
                        pk = (g, "k", c - 2) if c >= 2 else (g, "q", c + 2)
                    if pk in state:
                        Bld.wait("tensor", "tDVE", state[pk]["tDVE"])
                def mm(e, k=k, c=c, g=g, bank=bank, wsel=wsel):
                    return e.matmul(psB[:, bank: bank + 512],
                                    wsel[:, k * D + g * 128: k * D + g * 128 + 128],
                                    xbf[:, k * T + c * 512: k * T + c * 512 + 512],
                                    start=(k == 0), stop=(k == KT - 1))
                L = Bld.op("tensor", mm, incs=[("tPE", 1)] if k == KT - 1 else [])
                if k == KT - 1:
                    state[(g, which, c, "mm")] = L
                n_mm -= 1
            elif kind == "evict":
                _, which, c = steps.pop(0)
                bank = 1024 + (c % 2) * 512
                Bld.wait("vector", "tPE", state[(g, which, c, "mm")]["tPE"])
                def ev(e, c=c, bank=bank, dst=(qdst if which == "q" else kdst)):
                    return e.tensor_copy(dst[:, c * 512:(c + 1) * 512], psB[:, bank: bank + 512])
                state[(g, which, c)] = Bld.op("vector", ev, incs=[("tDVE", 1)])
            elif kind == "swap":
                _, which = steps.pop(0)
                dst = qdst if which == "q" else kdst
                sw = qsw if which == "q" else ksw
                sem = "sSwQ" if which == "q" else "sSwK"
                Bld.wait("sync", "tDVE", state[(g, which, 3)]["tDVE"])
                for (do, di) in [(0, 32), (32, 0), (64, 96), (96, 64)]:
                    dma("sync", sem, sw[do:do + 32, :], dst[di:di + 32, :])
                state[(g, which, "sw")] = Bld.now(sem)
            else:
                _, which = steps.pop(0)
                dst = qdst if which == "q" else kdst
                sw = qsw if which == "q" else ksw
                sem = "sSwQ" if which == "q" else "sSwK"
                Bld.wait("vector", sem, state[(g, which, "sw")])
                Bld.op("vector", lambda e, d=dst: e.tensor_mul(d[:], d[:], cosS[:]))
                Bld.op("vector", lambda e, sw=sw: e.tensor_mul(tmpv[:], sw[:], sinS[:]))
                state[(g, which + "ro")] = Bld.op(
                    "vector", lambda e, d=dst: e.tensor_add(d[:], d[:], tmpv[:]),
                    incs=[("tDVE", 1)])
            used += 1
        return used


    Bld.wait("tensor", "sGX", 32)
    Bld.wait("tensor", "sGV", 16)
    vproj = {}
    vevict = {}
    trick0 = build_trickle(0)
    for tt in range(NTT):
        bs = (tt % 2) * 1024
        if tt >= 2:
            Bld.wait("tensor", "tDVE", vevict[tt - 2]["tDVE"])
        for k in range(KT):
            for dc in range(2):
                def mm(e, k=k, dc=dc, bs=bs, tt=tt):
                    return e.matmul(psA[:, bs + dc * 512: bs + dc * 512 + 512],
                                    xbf[:, k * T + tt * 128: k * T + tt * 128 + 128],
                                    wvo[:, k * D + dc * 512: k * D + dc * 512 + 512],
                                    start=(k == 0), stop=(k == KT - 1))
                inc = [("tPE", 1)] if (k == KT - 1 and dc == 1) else []
                vproj[tt] = Bld.op("tensor", mm, incs=inc)
        if tt >= 8:
            Bld.wait("tensor", "sGQ", 16)
            Bld.wait("tensor", "sGK", 16)
            pump_trickle(0, trick0, 8)
        Bld.wait("vector", "tPE", vproj[tt]["tPE"])
        for dc in range(2):
            def ev(e, tt=tt, dc=dc, bs=bs):
                dst = vag[:, tt * 1040: (tt + 1) * 1040]
                dst = dst.rearrange("p (g m) -> p g m", g=16)[:, dc * 8:(dc + 1) * 8, 0:64]
                src = psA[:, bs + dc * 512: bs + dc * 512 + 512]
                src = src.rearrange("p (g m) -> p g m", g=8)
                return e.tensor_copy(dst, src)
            vevict[tt] = Bld.op("vector", ev, incs=[("tDVE", 1)] if dc == 1 else [])

    Bld.wait("gpsimd", "tPE", vproj[NTT - 1]["tPE"])
    dma("gpsimd", "sGV", wvo[:].rearrange("p (k d) -> p k d", k=KT),
        wop.rearrange("(k p) d -> p k d", p=128))


    while trick0:
        pump_trickle(0, trick0, 8)

    for g in range(8):
        qsb = qsbP[g % 2]
        ksb = ksbP[g % 2]
        trick = build_trickle(g + 1) if g < 7 else []
        exp_for_buf = {0: None, 1: None}
        av_for_buf = {0: None, 1: None}
        mask_for_buf = {0: None, 1: None}
        Bld.wait("tensor", "tDVE", state[(g, "kro")]["tDVE"])
        Bld.wait("tensor", "tDVE", state[(g, "qro")]["tDVE"])
        Bld.wait("tensor", "sGQ", 16)
        Bld.wait("tensor", "sGK", 16)
        for tq in range(4):
            nlt = 4 * (tq + 1)
            pending_av = None
            for lt in range(nlt):
                dt0 = max(0, lt * 128 - tq * 512)
                W = 512 - dt0
                buf = lt % 2
                is_diag = (dt0 > 0) or (lt == 4 * tq)

                if exp_for_buf[buf] is not None:
                    Bld.wait("tensor", "tACT", exp_for_buf[buf]["tACT"])
                elif buf == 1 and prev_avnm:
                    Bld.wait("tensor", "tDVE", prev_avnm)
                def mmE0(e, lt=lt, tq=tq, dt0=dt0, buf=buf, qsb=qsb, ksb=ksb):
                    return e.matmul(psA[:, buf * 1024 + dt0: buf * 1024 + 512],
                                    ksb[0:64, lt * 128: lt * 128 + 128],
                                    qsb[0:64, tq * 512 + dt0: (tq + 1) * 512],
                                    start=True, stop=True)
                Bld.op("tensor", mmE0)
                def mmE1(e, lt=lt, tq=tq, dt0=dt0, buf=buf, qsb=qsb, ksb=ksb):
                    return e.matmul(psA[:, buf * 1024 + 512 + dt0: buf * 1024 + 1024],
                                    ksb[64:128, lt * 128: lt * 128 + 128],
                                    qsb[64:128, tq * 512 + dt0: (tq + 1) * 512],
                                    start=True, stop=True)
                eL = Bld.op("tensor", mmE1, incs=[("tPE", 1)])

                Bld.wait("scalar", "tPE", eL["tPE"])
                if av_for_buf[buf] is not None:
                    Bld.wait("scalar", "tPE", av_for_buf[buf]["tPE"])
                if mask_for_buf[buf] is not None:
                    Bld.wait("scalar", "tDVE", mask_for_buf[buf]["tDVE"])
                def ex(e, dt0=dt0, buf=buf):
                    return e.activation(
                        ppair[:, buf * 1024: buf * 1024 + 1024].rearrange(
                            "p (h w) -> p h w", h=2)[:, :, dt0:512],
                        psA[:, buf * 1024: buf * 1024 + 1024].rearrange(
                            "p (h w) -> p h w", h=2)[:, :, dt0:512],
                        AF.Exp)
                expL = Bld.op("scalar", ex, incs=[("tACT", 1)])
                exp_for_buf[buf] = expL

                maskL = None
                if is_diag:
                    Bld.wait("vector", "tACT", expL["tACT"])
                    def mk(e, dt0=dt0, buf=buf):
                        view = ppair[:, buf * 1024: buf * 1024 + 1024].rearrange(
                            "p (h w) -> p h w", h=2)[:, :, dt0:dt0 + 128]
                        return e.tensor_tensor(
                            view, view,
                            triS[:].unsqueeze(1).broadcast_to([128, 2, 128]),
                            ALU.mult)
                    maskL = Bld.op("vector", mk, incs=[("tDVE", 1)])
                    mask_for_buf[buf] = maskL

                def emit_av(lt, dt0, buf, expL, maskL):
                    Bld.wait("tensor", "tACT", expL["tACT"])
                    if maskL is not None:
                        Bld.wait("tensor", "tDVE", maskL["tDVE"])
                    if lt == 0 and prev_avev_dve:
                        Bld.wait("tensor", "tDVE", prev_avev_dve)
                    def av0(e, g=g, lt=lt, dt0=dt0, buf=buf, first=(lt == 0)):
                        return e.matmul(psB[0:65, dt0:512],
                                        vag[:, lt * 1040 + (2 * g) * 65: lt * 1040 + (2 * g) * 65 + 65],
                                        ppair[:, buf * 1024 + dt0: buf * 1024 + 512],
                                        start=first, stop=True)
                    Bld.op("tensor", av0)
                    def av1(e, g=g, lt=lt, dt0=dt0, buf=buf, first=(lt == 0)):
                        return e.matmul(psB[0:65, 512 + dt0: 1024],
                                        vag[:, lt * 1040 + (2 * g + 1) * 65: lt * 1040 + (2 * g + 1) * 65 + 65],
                                        ppair[:, buf * 1024 + 512 + dt0: buf * 1024 + 1024],
                                        start=first, stop=True)
                    L = Bld.op("tensor", av1, incs=[("tPE", 1)])
                    av_for_buf[buf] = L
                    return L

                if pending_av is not None:
                    avL = emit_av(*pending_av)
                pending_av = (lt, dt0, buf, expL, maskL)

                if trick:
                    pump_trickle(g + 1, trick, 4 if tq < 2 else 1)

            if pending_av is not None:
                avL = emit_av(*pending_av)
            Bld.wait("vector", "tPE", avL["tPE"])
            avevV = Bld.op("vector", lambda e: e.tensor_copy(scr[0:64, 0:1024], psB[0:64, 0:1024]),
                           incs=[("tDVE", 1)])
            Bld.wait("scalar", "tPE", avL["tPE"])
            Bld.op("scalar", lambda e: e.activation(scr[64:65, 0:1024], psB[64:65, 0:1024], AF.Ln))
            recL = Bld.op("scalar", lambda e: e.activation(
                scr[64:65, 1024:1536].bitcast(BF16), scr[64:65, 0:1024], AF.Exp, scale=-1.0),
                incs=[("tACT", 1)])
            if trick:
                pump_trickle(g + 1, trick, 8)
            Bld.wait("tensor", "tACT", recL["tACT"])
            for c in range(2):
                def bc(e, c=c):
                    return e.matmul(psA[0:64, 1024 + c * 512: 1024 + (c + 1) * 512],
                                    onesr[64:65, 0:64],
                                    scr[64:65, 1024:1536].bitcast(BF16)[:, c * 512:(c + 1) * 512],
                                    start=True, stop=True)
                bcL = Bld.op("tensor", bc, incs=[("tPE", 1)] if c == 1 else [])
            Bld.wait("vector", "tPE", bcL["tPE"])
            off = g * T + tq * 512
            Bld.op("vector", lambda e, off=off: e.tensor_mul(
                avn[0:64, off: off + 512], scr[0:64, 0:512], psA[0:64, 1024:1536]))
            avnmL = Bld.op("vector", lambda e, off=off: e.tensor_mul(
                avn[64:128, off: off + 512], scr[0:64, 512:1024], psA[0:64, 1536:2048]),
                incs=[("tDVE", 1)])
            prev_avnm = avnmL["tDVE"]
            prev_avev_dve = avevV["tDVE"]
            exp_for_buf = {0: None, 1: None}
            av_for_buf = {0: None, 1: None}
            mask_for_buf = {0: None, 1: None}

        while trick:
            pump_trickle(g + 1, trick, 8)
        if g == 6:
            Bld.wait("gpsimd", "tDVE", state[(7, "kro")]["tDVE"])
            for j in range(4):
                def ldw1(e, j=j):
                    return e.dma_start(
                        out=H["w1"][:].rearrange("p (k h) -> p k h", k=KT)[:, 2 * j: 2 * j + 2, :],
                        in_=w1p.rearrange("(k p) h -> p k h", p=128)[:, 2 * j: 2 * j + 2, :])
                Bld.op("gpsimd", ldw1, incs=[("sGW1", 16)])
            for j in range(4):
                def ldw2(e, j=j):
                    return e.dma_start(
                        out=H["w2"][:].rearrange("p (k d) -> p k d", k=HKT)[:, 8 * j: 8 * j + 8, :],
                        in_=w2p.rearrange("(k p) d -> p k d", p=128)[:, 8 * j: 8 * j + 8, :])
                Bld.op("gpsimd", ldw2, incs=[("sGW2", 16)])

    attn_done_dve = prev_avnm
    attn_done_act = Bld.now("tACT")

    cut = Bld.mark()
    attn.close()

    ffn = ExitStack()
    F_ = ffn.enter_context
    w1q8 = F_(nc.sbuf_tensor("w1q8", [128, KT * H4], F8))
    w2q8 = F_(nc.sbuf_tensor("w2q8", [128, HKT * D], F8))
    H["w1"] = w1q8
    H["w2"] = w2q8
    h8 = F_(nc.sbuf_tensor("h8", [128, HKT * 512], F8))
    sslab = F_(nc.sbuf_tensor("sslab", [128, 2 * KT * 512], BF16))
    x1q8 = F_(nc.sbuf_tensor("x1q8", [128, KT * 512], F8))
    xre = F_(nc.sbuf_tensor("xre", [128, 2 * 512], F32))
    mkr = F_(nc.sbuf_tensor("mkr", [128, 8 * 512], F8))
    sqb = F_(nc.sbuf_tensor("sqb", [128, 8 * 512], BF16))
    h16 = F_(nc.sbuf_tensor("h16", [128, 2 * 512], F16))
    y16 = h16
    ymul = F_(nc.sbuf_tensor("ymul", [128, 512], BF16))
    ost = F_(nc.sbuf_tensor("ost", [128, 2 * 512], F32))


    xre_idx = {}
    mkr_idx = {}
    sadd_hist = {}
    proj_hist = {}
    yev_hist = {}
    out_done = {}
    FS = {}

    def slab(tc, dm):
        off = (tc % 2) * KT * 512 + dm * 512
        return sslab[:, off: off + 512]

    def emit_D_proj(tc, dm):
        i = tc * KT + dm
        if i - 2 in sadd_hist:
            Bld.wait("sync", "tDVE", sadd_hist[i - 2]["tDVE"])
        xsem = f"xr{i % 2}"
        def ldx(e, dm=dm, tc=tc, i=i):
            return e.dma_start(
                out=xre[:, (i % 2) * 512:(i % 2 + 1) * 512],
                in_=xT.rearrange("(k p) t -> p k t", p=128)[:, dm, tc * 512:(tc + 1) * 512])
        Bld.op("sync", ldx, incs=[(xsem, 16)])
        xre_idx[i] = (xsem, Bld.now(xsem))

        if tc == 0 and dm == 0:
            Bld.wait("tensor", "sGV", 32)
            Bld.wait("tensor", "tDVE", attn_done_dve)
        if i >= 2:
            Bld.wait("tensor", "tDVE", sadd_hist[i - 2]["tDVE"])
        projL = None
        for k in range(KT):
            def mm(e, k=k, dm=dm, tc=tc):
                return e.matmul(psA[:, (dm % 2) * 512:(dm % 2) * 512 + 512],
                                wvo[:, k * D + dm * 128: k * D + dm * 128 + 128],
                                avn[:, k * T + tc * 512: k * T + tc * 512 + 512],
                                start=(k == 0), stop=(k == KT - 1))
            projL = Bld.op("tensor", mm, incs=[("tPE", 1)] if k == KT - 1 else [])
        proj_hist[i] = projL

    def emit_D_sadd(tc, dm):
        i = tc * KT + dm
        Bld.wait("vector", "tPE", proj_hist[i]["tPE"])
        Bld.wait("vector", *xre_idx[i])
        def sadd(e, dm=dm, i=i, tc=tc):
            return e.tensor_add(slab(tc, dm),
                                psA[:, (dm % 2) * 512:(dm % 2) * 512 + 512],
                                xre[:, (i % 2) * 512:(i % 2 + 1) * 512])
        sadd_hist[i] = Bld.op("vector", sadd, incs=[("tDVE", 1)])

    def emit_D_chunk(tc, dm):
        emit_D_proj(tc, dm)
        emit_D_sadd(tc, dm)

    def emit_D_partials(tc, dm, slot_gate=None):
        i = tc * KT + dm
        if slot_gate is not None:
            Bld.wait("vector", "tPE", slot_gate["tPE"])
        def csq(e, dm=dm, tc=tc):
            return e.tensor_tensor(sqb[:, dm * 512:(dm + 1) * 512],
                                   slab(tc, dm), slab(tc, dm), ALU.mult)
        FS[(tc, "sq", dm)] = Bld.op("vector", csq, incs=[("tDVE", 1)])

    def emit_mask_load(tc, dm):
        i = tc * KT + dm
        if i - 8 in yev_hist:
            Bld.wait("sync", "tDVE", yev_hist[i - 8]["tDVE"])
        msem = f"mk{i % 8}"
        def ldm(e, dm=dm, tc=tc, i=i):
            return e.dma_start(
                out=mkr[:, (i % 8) * 512:(i % 8 + 1) * 512],
                in_=maskT.rearrange("(k p) t -> p k t", p=128)[:, dm, tc * 512:(tc + 1) * 512])
        Bld.op("sync", ldm, incs=[(msem, 16)])
        mkr_idx[i] = (msem, Bld.now(msem))

    def emit_D_stats(tc, gate):
        statsL = None
        for dm in range(KT):
            Bld.wait("tensor", "tDVE", FS[(tc, "sq", dm)]["tDVE"])
            if dm == 0 and gate:
                Bld.wait("tensor", "tDVE", gate)
            def st1(e, dm=dm, tc=tc):
                return e.matmul(psB[0:1, 1024:1536], onesb[:, 0:1],
                                slab(tc, dm),
                                start=(dm == 0), stop=(dm == KT - 1))
            Bld.op("tensor", st1)
            def st2(e, dm=dm):
                return e.matmul(psB[0:1, 1536:2048], onesb[:, 0:1],
                                sqb[:, dm * 512:(dm + 1) * 512],
                                start=(dm == 0), stop=(dm == KT - 1))
            statsL = Bld.op("tensor", st2, incs=[("tPE", 1)])
        FS[(tc, "stats")] = statsL

    def emit_rows(statsL, rows_ps, r):
        Bld.wait("vector", "tPE", statsL["tPE"])
        Bld.op("vector", lambda e: e.tensor_scalar_mul(
            scr[r:r + 1, 0:512], rows_ps[0:1, 1024:1536], 1.0 / D))
        Bld.op("vector", lambda e: e.tensor_mul(
            scr[r:r + 1, 1536:2048], scr[r:r + 1, 0:512], scr[r:r + 1, 0:512]))
        vrow = Bld.op("vector", lambda e: e.scalar_tensor_tensor(
            scr[r:r + 1, 512:1024], rows_ps[0:1, 1536:2048], 1.0 / D, scr[r:r + 1, 1536:2048],
            ALU.mult, ALU.subtract), incs=[("tDVE", 1)])
        mbfL = Bld.op("vector", lambda e: e.tensor_copy(
            scr[r:r + 1, 1280:1536].bitcast(BF16), scr[r:r + 1, 0:512]), incs=[("tDVE", 1)])
        Bld.wait("scalar", "tDVE", vrow["tDVE"])
        Bld.op("scalar", lambda e: e.activation(
            scr[r:r + 1, 1536:2048], scr[r:r + 1, 512:1024], AF.Ln, bias=EPS))
        rstdL = Bld.op("scalar", lambda e: e.activation(
            scr[r:r + 1, 1024:1280].bitcast(BF16), scr[r:r + 1, 1536:2048], AF.Exp, scale=-0.5),
            incs=[("tACT", 1)])
        return vrow, mbfL, rstdL

    def emit_bcast(r, mbfL, rstdL, bc_gate=None):
        Bld.wait("tensor", "tACT", rstdL["tACT"])
        Bld.wait("tensor", "tDVE", mbfL["tDVE"])
        if bc_gate:
            Bld.wait("tensor", "tDVE", bc_gate)
        Bld.op("tensor", lambda e: e.matmul(
            psA[:, 1024:1536], onesr[r:r + 1, 0:128], scr[r:r + 1, 1280:1536].bitcast(BF16),
            start=True, stop=True))
        bcL = Bld.op("tensor", lambda e: e.matmul(
            psA[:, 1536:2048], onesr[r:r + 1, 0:128], scr[r:r + 1, 1024:1280].bitcast(BF16),
            start=True, stop=True),
            incs=[("tPE", 1)])
        return bcL

    Bld.wait("sync", "tDVE", attn_done_dve)
    for dm in range(KT):
        emit_D_chunk(0, dm)
        emit_D_partials(0, dm)
        emit_mask_load(0, dm)
    emit_D_stats(0, None)

    def emit_x1n(tcx, bcL, interleave):
        x1n_k = {}
        x1nV = None
        Bld.wait("vector", "tPE", bcL["tPE"])
        for dm in range(KT):
            def n1(e, dm=dm, tc=tcx):
                return e.tensor_sub(slab(tc, dm),
                                    slab(tc, dm),
                                    psA[:, 1024:1536])
            Bld.op("vector", n1)
            def n2(e, dm=dm, tc=tcx):
                return e.tensor_mul(slab(tc, dm),
                                    slab(tc, dm),
                                    psA[:, 1536:2048])
            x1nV = Bld.op("vector", n2, incs=[("tDVE", 1)])
            Bld.wait("scalar", "tDVE", x1nV["tDVE"])
            def n3(e, dm=dm, tc=tcx):
                return e.copy(x1q8[:, dm * 512:(dm + 1) * 512],
                              slab(tc, dm))
            x1n_k[dm] = Bld.op("scalar", n3, incs=[("tACT", 1)])
            if interleave:
                emit_D_chunk(tcx + 1, dm)
        return x1n_k, x1nV

    _, mb0, rs0 = emit_rows(FS[(0, "stats")], psB, 0)
    bc1L = emit_bcast(0, mb0, rs0)
    x1n_k_cur, x1nV_cur = emit_x1n(0, bc1L, True)
    prev_x2_dve = None

    for tc in range(TC):
        hev = {}
        for hm in range(HKT):
            if tc == 0 and hm == 0:
                Bld.wait("tensor", "sGW1", 64)
            if hm == 0 and tc > 0:
                Bld.wait("tensor", "tDVE", yev_hist[tc * KT - 1]["tDVE"])
            if hm >= 4:
                Bld.wait("tensor", "tDVE", hev[hm - 4]["tDVE"])
            mm1L = None
            for kb in range(KT // 2):
                if hm == 0:
                    Bld.wait("tensor", "tACT", x1n_k_cur[2 * kb + 1]["tACT"])
                def mm(e, kb=kb, hm=hm):
                    lhsT = w1q8[:].rearrange("p (k h) -> p k h", k=KT)[
                        :, 2 * kb: 2 * kb + 2, hm * 128:(hm + 1) * 128]
                    rhs = x1q8[:].rearrange("p (k t) -> p k t", k=KT)[
                        :, 2 * kb: 2 * kb + 2, :]
                    return e.matmul(psB[:, (hm % 4) * 512:(hm % 4) * 512 + 512],
                                    lhsT, rhs, start=(kb == 0), stop=(kb == KT // 2 - 1),
                                    perf_mode=PM.DoubleRow)
                mm1L = Bld.op("tensor", mm, incs=[("tPE", 1)] if kb == KT // 2 - 1 else [])
            Bld.wait("scalar", "tPE", mm1L["tPE"])
            if hm >= 2:
                Bld.wait("scalar", "tDVE", hev[hm - 2]["tDVE"])
            def he1(e, hm=hm):
                return e.activation(h16[:, (hm % 2) * 512:(hm % 2) * 512 + 512],
                                    psB[:, (hm % 4) * 512:(hm % 4) * 512 + 512],
                                    AF.Relu, bias=b1f[:, hm:hm + 1])
            he1L = Bld.op("scalar", he1, incs=[("tACT", 1)])
            Bld.wait("vector", "tACT", he1L["tACT"])
            def he2(e, hm=hm):
                return e.tensor_copy(h8[:, hm * 512:(hm + 1) * 512],
                                     h16[:, (hm % 2) * 512:(hm % 2) * 512 + 512])
            hev[hm] = Bld.op("vector", he2, incs=[("tDVE", 1)])

        stats2L = None
        pend_sq = []
        psA_rows_free = prev_x2_dve if prev_x2_dve is not None else x1nV_cur["tDVE"]

        def emit_stats2(tc, dm, sq2L):
            Bld.wait("tensor", "tDVE", sq2L["tDVE"])
            if dm == 0:
                Bld.wait("tensor", "tDVE", psA_rows_free)
            def st1b(e, dm=dm, tc=tc):
                return e.matmul(psA[0:1, 1024:1536], onesb[:, 0:1],
                                slab(tc, dm),
                                start=(dm == 0), stop=(dm == KT - 1))
            Bld.op("tensor", st1b)
            def st2b(e, dm=dm):
                return e.matmul(psA[0:1, 1536:2048], onesb[:, 0:1],
                                sqb[:, dm * 512:(dm + 1) * 512],
                                start=(dm == 0), stop=(dm == KT - 1))
            L = Bld.op("tensor", st2b, incs=[("tPE", 1)])
            if tc < TC - 1:
                emit_D_partials(tc + 1, dm, slot_gate=L)
            return L

        for dm in range(KT):
            i = tc * KT + dm
            if tc == 0 and dm == 0:
                Bld.wait("tensor", "sGW2", 64)
            if dm == 0:
                Bld.wait("tensor", "tDVE", hev[HKT - 1]["tDVE"])
            if i >= 2:
                Bld.wait("tensor", "tDVE", yev_hist[i - 2]["tDVE"])
            mm2L = None
            for kb in range(HKT // 2):
                def mm(e, kb=kb, dm=dm):
                    lhsT = w2q8[:].rearrange("p (k d) -> p k d", k=HKT)[
                        :, 2 * kb: 2 * kb + 2, dm * 128:(dm + 1) * 128]
                    rhs = h8[:].rearrange("p (k t) -> p k t", k=HKT)[
                        :, 2 * kb: 2 * kb + 2, :]
                    return e.matmul(psB[:, 1024 + (dm % 2) * 512: 1024 + (dm % 2) * 512 + 512],
                                    lhsT, rhs, start=(kb == 0), stop=(kb == HKT // 2 - 1),
                                    perf_mode=PM.DoubleRow)
                mm2L = Bld.op("tensor", mm, incs=[("tPE", 1)] if kb == HKT // 2 - 1 else [])
            Bld.wait("scalar", "tPE", mm2L["tPE"])
            if dm >= 2:
                Bld.wait("scalar", "tDVE", yev_hist[tc * KT + dm - 2]["tDVE"])
            def ye1(e, dm=dm):
                return e.activation(y16[:, (dm % 2) * 512:(dm % 2) * 512 + 512],
                                    psB[:, 1024 + (dm % 2) * 512: 1024 + (dm % 2) * 512 + 512],
                                    AF.Identity, bias=b2f[:, dm:dm + 1])
            ye1L = Bld.op("scalar", ye1, incs=[("tACT", 1)])
            Bld.wait("vector", "tACT", ye1L["tACT"])
            Bld.wait("vector", *mkr_idx[i])
            def ye2(e, dm=dm, i=i):
                return e.tensor_mul(ymul[:],
                                    y16[:, (dm % 2) * 512:(dm % 2) * 512 + 512],
                                    mkr[:, (i % 8) * 512:(i % 8 + 1) * 512])
            Bld.op("vector", ye2)
            def ye3(e, dm=dm, tc=tc):
                return e.tensor_add(slab(tc, dm),
                                    slab(tc, dm),
                                    ymul[:])
            yev_hist[i] = Bld.op("vector", ye3, incs=[("tDVE", 1)])
            if tc < TC - 1:
                emit_mask_load(tc + 1, dm)
            def csq2(e, dm=dm, tc=tc):
                return e.tensor_tensor(sqb[:, dm * 512:(dm + 1) * 512],
                                       slab(tc, dm),
                                       slab(tc, dm), ALU.mult)
            sq2L = Bld.op("vector", csq2, incs=[("tDVE", 1)])
            pend_sq.append((dm, sq2L))
            if dm > 0:
                stats2L = emit_stats2(tc, *pend_sq.pop(0))
        stats2L = emit_stats2(tc, *pend_sq.pop(0))

        if tc < TC - 1:
            emit_D_stats(tc + 1, yev_hist[tc * KT + KT - 1]["tDVE"])

        vr2, mb2, rs2 = emit_rows(stats2L, psA, 32)
        if tc < TC - 1:
            _, mb1, rs1 = emit_rows(FS[(tc + 1, "stats")], psB, 0)
            bc1L = emit_bcast(0, mb1, rs1, bc_gate=vr2["tDVE"])
            if tc + 2 <= TC - 1:
                emit_D_proj(tc + 2, 0)
                emit_D_proj(tc + 2, 1)
            x1n_k_nxt, x1nV_nxt = emit_x1n(tc + 1, bc1L, False)
            bc2L = emit_bcast(32, mb2, rs2, bc_gate=x1nV_nxt["tDVE"])
        else:
            bc2L = emit_bcast(32, mb2, rs2)

        Bld.wait("vector", "tPE", bc2L["tPE"])
        x2L = None
        for dm in range(KT):
            i = tc * KT + dm
            def o1(e, dm=dm, tc=tc):
                return e.tensor_sub(slab(tc, dm),
                                    slab(tc, dm),
                                    psA[:, 1024:1536])
            Bld.op("vector", o1)
            if i >= 2:
                Bld.wait("vector", *out_done[i - 2])
            def o2(e, dm=dm, i=i, tc=tc):
                return e.tensor_mul(ost[:, (i % 2) * 512:(i % 2 + 1) * 512],
                                    slab(tc, dm),
                                    psA[:, 1536:2048])
            x2L = Bld.op("vector", o2, incs=[("tDVE", 1)])
            Bld.wait("sync", "tDVE", x2L["tDVE"])
            def so(e, dm=dm, tc=tc, i=i):
                return e.dma_start(
                    out=yT.rearrange("(k p) t -> p k t", p=128)[:, dm, tc * 512:(tc + 1) * 512],
                    in_=ost[:, (i % 2) * 512:(i % 2 + 1) * 512])
            osem = f"ot{i % 2}"
            Bld.op("sync", so, incs=[(osem, 16)])
            out_done[i] = (osem, Bld.now(osem))
            if tc + 2 <= TC - 1:
                emit_D_sadd(tc + 2, dm)
                if dm + 2 < KT:
                    emit_D_proj(tc + 2, dm + 2)

        if tc < TC - 1:
            x1n_k_cur, x1nV_cur = x1n_k_nxt, x1nV_nxt
        prev_x2_dve = x2L["tDVE"]

    for j in range(4):
        Bld.wait("sync", f"ot{j}", Bld.now(f"ot{j}"))

    end = Bld.mark()
    with nc.Block("attn") as block:
        @block.tensor
        def _(e):
            Bld.emit("tensor", e, sems, {k: 0 for k in cut}, cut)

        @block.scalar
        def _(e):
            Bld.emit("scalar", e, sems, {k: 0 for k in cut}, cut)

        @block.vector
        def _(e):
            Bld.emit("vector", e, sems, {k: 0 for k in cut}, cut)

        @block.sync
        def _(e):
            Bld.emit("sync", e, sems, {k: 0 for k in cut}, cut)

        @block.gpsimd
        def _(e):
            Bld.emit("gpsimd", e, sems, {k: 0 for k in cut}, cut)

    with nc.Block("ffn") as block:
        @block.tensor
        def _(e):
            Bld.emit("tensor", e, sems, cut, end)

        @block.scalar
        def _(e):
            Bld.emit("scalar", e, sems, cut, end)

        @block.vector
        def _(e):
            Bld.emit("vector", e, sems, cut, end)

        @block.sync
        def _(e):
            Bld.emit("sync", e, sems, cut, end)

        @block.gpsimd
        def _(e):
            Bld.emit("gpsimd", e, sems, cut, end)

    ffn.close()
    outer.close()
    return nc


_ROWPERM = np.concatenate(
    [h * 64 + np.concatenate([np.arange(0, 64, 2), np.arange(1, 64, 2)])
     for h in range(NH)])


def kernel(L0_attn_x, L0_attn_wq, L0_attn_wk, L0_attn_wv, L0_attn_wo,
           L0_ffn_w1, L0_ffn_b1, L0_ffn_w2, L0_ffn_b2):
    x = np.asarray(L0_attn_x, np.float32)
    wq = np.asarray(L0_attn_wq, np.float32)
    wk = np.asarray(L0_attn_wk, np.float32)
    wv = np.asarray(L0_attn_wv, np.float32)
    wo = np.asarray(L0_attn_wo, np.float32)
    w1 = np.asarray(L0_ffn_w1, np.float32)
    b1 = np.asarray(L0_ffn_b1, np.float32)
    w2 = np.asarray(L0_ffn_w2, np.float32)
    b2 = np.asarray(L0_ffn_b2, np.float32)

    cosA, sinB, tri, maskv = _host_tables()
    wqp = np.ascontiguousarray(wq[_ROWPERM].T)
    wkp = np.ascontiguousarray(wk[_ROWPERM].T)
    wvp = np.ascontiguousarray(wv.T)
    wop = np.ascontiguousarray(wo.T)
    w1p = np.ascontiguousarray(w1.T)
    w2p = np.ascontiguousarray(w2.T)
    b1m = np.ascontiguousarray(b1.reshape(32, 128).T)
    b2m = np.ascontiguousarray(b2.reshape(8, 128).T)
    cstv = np.zeros((128, 8), np.float32)
    cstv[:, 0] = EPS

    nc = build_nc()
    in_maps = []
    for b in range(B):
        in_maps.append({
            "xT": np.ascontiguousarray(x[b].T),
            "wqp": wqp, "wkp": wkp, "wvp": wvp, "wop": wop,
            "w1p": w1p, "w2p": w2p, "b1m": b1m, "b2m": b2m,
            "cosA": cosA, "sinB": sinB, "triI": tri,
            "maskT": np.ascontiguousarray(maskv[b].T),
            "cstI": cstv,
        })
    res = run_bass_kernel_spmd(nc, in_maps, core_ids=list(range(B)))
    out = np.stack([np.asarray(r["yT"], np.float32).T for r in res.results], axis=0)
    return out
